# version 01
# baseline (speedup 1.0000x reference)
import contextlib
import numpy as np
import concourse.bass as bass
import concourse.mybir as mybir
from concourse.bass_utils import run_bass_kernel_spmd

F32 = mybir.dt.float32
AF = mybir.ActivationFunctionType
OP = mybir.AluOpType
AX = mybir.AxisListType.X

P = 128
DM = 1024
T = 2048
S = 4096
EPS_LN = 1e-5
EPS_ATTN = 1e-3

ENGS = ("sync", "tensor", "vector", "scalar", "gpsimd")


class Prog:
    def __init__(self, nc, stack):
        self.nc = nc
        self.ops = {e: [] for e in ENGS}
        self.sems = {}
        self.stack = stack

    def _sem(self, name):
        if name not in self.sems:
            h = self.stack.enter_context(self.nc.semaphore(name))
            self.sems[name] = [h, 0]
        return self.sems[name]

    def emit(self, eng, fn, sig=None, step=1):
        ev = None
        if sig is not None:
            s = self._sem(sig)
            s[1] += step
            ev = (s[0], s[1])
            self.ops[eng].append((fn, s[0], step))
        else:
            self.ops[eng].append((fn, None, 0))
        return ev

    def vemit(self, fn, sig="vs"):
        ev = self.emit("vector", fn, sig=sig)
        self.wait("vector", ev)
        return ev

    def semit(self, fn, sig="ss_"):
        ev = self.emit("scalar", fn, sig=sig)
        self.wait("scalar", ev)
        return ev

    def dma(self, eng, out, in_, sig):
        return self.emit(eng, lambda h, o=out, i=in_: h.dma_start(out=o, in_=i),
                         sig=sig, step=16)

    def wait(self, eng, ev):
        if ev is None:
            return
        sem, val = ev
        self.ops[eng].append((("wait", sem, val), None, 0))

    def cur(self, name):
        s = self._sem(name)
        return (s[0], s[1])

    def build(self, block):
        for eng in ENGS:
            if not self.ops[eng]:
                continue
            deco = getattr(block, eng)
            lst = self.ops[eng]

            def body(h, lst=lst):
                for fn, sem, step in lst:
                    if isinstance(fn, tuple) and fn[0] == "wait":
                        h.wait_ge(fn[1], fn[2])
                    else:
                        inst = fn(h)
                        if sem is not None:
                            inst.then_inc(sem, step)
            deco(body)


def ln_token_tile(p, xin, zout, sqs, stats):
    p.vemit(lambda v: v.reduce_sum(stats[:, 0:1], xin, axis=AX))
    p.vemit(lambda v: v.tensor_scalar(stats[:, 1:2], stats[:, 0:1],
                                      1.0 / DM, None, OP.mult))
    p.vemit(lambda v: v.tensor_scalar(zout, xin, stats[:, 1:2], None,
                                      OP.subtract))
    p.vemit(lambda v: v.scalar_tensor_tensor(
        sqs, zout, 1.0, zout, OP.mult, OP.mult, accum_out=stats[:, 2:3]))
    p.emit("vector", lambda v: v.tensor_scalar(stats[:, 3:4], stats[:, 2:3],
                                               1.0 / DM, EPS_LN, OP.mult, OP.add),
           sig="ln_v")
    p.wait("scalar", p.cur("ln_v"))
    evs = p.emit("scalar", lambda s: s.activation(stats[:, 4:5], stats[:, 3:4],
                                                  AF.Sqrt), sig="ln_s")
    p.wait("vector", evs)
    p.vemit(lambda v: v.reciprocal(stats[:, 5:6], stats[:, 4:5]))
    return p.emit("vector", lambda v: v.tensor_scalar(zout, zout, stats[:, 5:6],
                                                      None, OP.mult), sig="ln_z")


def ln_feat_tb(p, pys, xsrc_chunks, zdst_chunks, onescol, ones1, sq2, lnst,
               gam=None, bet=None, eps=EPS_LN):
    if p._sem("lf_z")[1] > 0:
        p.wait("tensor", p.cur("lf_z"))
    if p._sem("lf_r")[1] > 0:
        p.wait("scalar", p.cur("lf_r"))
        p.wait("tensor", p.cur("lf_r"))
    msb = lnst[0:1, 0:512]
    qsb = lnst[0:1, 512:1024]
    t2 = sq2[0:1, 0:512]
    for dc in range(8):
        ev = p.emit("tensor", lambda t, dc=dc: t.matmul(
            pys[0][0:1, :], onescol[:, :], xsrc_chunks[dc],
            start=(dc == 0), stop=(dc == 7)), sig="lf_s" if dc == 7 else None)
    evsum = ev
    qmm = []
    for dc in range(8):
        if dc >= 2:
            p.wait("scalar", qmm[dc - 2])
        eva = p.emit("scalar", lambda s, dc=dc: s.activation(
            sq2[:, (dc % 2) * 512:(dc % 2 + 1) * 512], xsrc_chunks[dc], AF.Square),
            sig="lf_a")
        p.wait("tensor", eva)
        ev = p.emit("tensor", lambda t, dc=dc: t.matmul(
            pys[1][0:1, :], onescol[:, :], sq2[:, (dc % 2) * 512:(dc % 2 + 1) * 512],
            start=(dc == 0), stop=(dc == 7)), sig="lf_q")
        qmm.append(ev)
    evq = ev
    p.wait("vector", evsum)
    p.wait("vector", evq)
    p.vemit(lambda v: v.tensor_scalar(msb, pys[0][0:1, :], 1.0 / DM, None, OP.mult))
    p.vemit(lambda v: v.tensor_scalar(qsb, pys[1][0:1, :], 1.0 / DM, None, OP.mult))
    p.vemit(lambda v: v.tensor_mul(t2, msb, msb))
    p.vemit(lambda v: v.tensor_sub(qsb, qsb, t2))
    ev = p.emit("vector", lambda v: v.tensor_scalar(qsb, qsb, 1.0, eps, OP.mult, OP.add),
                sig="lf_v")
    p.wait("scalar", ev)
    ev = p.emit("scalar", lambda s: s.activation(t2, qsb, AF.Sqrt), sig="lf_sq")
    p.wait("vector", ev)
    ev = p.vemit(lambda v: v.reciprocal(qsb, t2), sig="lf_r")
    p.wait("tensor", ev)
    evm = p.emit("tensor", lambda t: t.matmul(pys[2][:, :], ones1[:, :], msb,
                                              start=True, stop=True), sig="lf_b")
    evr = p.emit("tensor", lambda t: t.matmul(pys[3][:, :], ones1[:, :], qsb,
                                              start=True, stop=True), sig="lf_b")
    p.wait("vector", evm)
    p.wait("vector", evr)
    out_evs = []
    for dc in range(8):
        p.emit("vector", lambda v, dc=dc: v.tensor_sub(zdst_chunks[dc],
                                                       xsrc_chunks[dc], pys[2][:, :]))
        if gam is None:
            ev = p.emit("vector", lambda v, dc=dc: v.tensor_mul(
                zdst_chunks[dc], zdst_chunks[dc], pys[3][:, :]), sig="lf_z")
        else:
            p.emit("vector", lambda v, dc=dc: v.tensor_mul(
                zdst_chunks[dc], zdst_chunks[dc], pys[3][:, :]))
            ev = p.emit("vector", lambda v, dc=dc: v.tensor_scalar(
                zdst_chunks[dc], zdst_chunks[dc], gam[dc], bet[dc],
                OP.mult, OP.add), sig="lf_z")
        out_evs.append(ev)
    return out_evs


def build_conv():
    nc = bass.Bass("TRN2", target_bir_lowering=False, debug=False)
    xb = nc.dram_tensor("xb", [2 * T, DM], F32, kind="ExternalInput").ap()
    fpad = nc.dram_tensor("fpad", [2303], F32, kind="ExternalInput").ap()
    x1 = nc.dram_tensor("x1", [T, DM], F32, kind="ExternalOutput").ap()
    NT = 32
    with contextlib.ExitStack() as st:
        sb = lambda n, s, d=F32: st.enter_context(nc.sbuf_tensor(n, s, d))
        z0t = sb("z0t", [P, NT * DM])
        xt = sb("xt", [P, 2 * DM])
        sqs = sb("sqs", [P, DM])
        stats = sb("stats", [P, 8])
        hank = sb("hank", [P, 17 * P])
        f2 = sb("f2", [P, 17 * P])
        osb = sb("osb", [P, 2 * DM])
        pys = [st.enter_context(nc.psum_tensor(f"py{i}", [P, 512], F32))
               for i in range(4)]
        p = Prog(nc, st)

        for q in range(17):
            p.dma("sync", hank[:, q * P:(q + 1) * P],
                  bass.AP(fpad.tensor, 128 * q, [[1, 128], [1, 128]]), "hk")
        p.wait("vector", p.cur("hk"))
        for q in range(17):
            p.emit("vector", lambda v, q=q: v.tensor_copy(
                f2[:, q * P:(q + 1) * P], hank[:, q * P:(q + 1) * P][:, ::-1]),
                sig="f2r")
        zdone = []
        for j in range(NT):
            buf = xt[:, (j % 2) * DM:(j % 2 + 1) * DM]
            if j >= 2:
                p.wait("sync", zdone[j - 2])
            evd = p.dma("sync", buf, xb[j * P:(j + 1) * P, :], "xld")
            p.wait("vector", evd)
            ev = ln_token_tile(p, buf, z0t[:, j * DM:(j + 1) * DM], sqs[:, :], stats)
            zdone.append(ev)
        p.wait("tensor", p.cur("f2r"))
        mmev = {}
        evacev = {}
        stev = []
        for j in range(16):
            qmax = min(17, NT - j)
            p.wait("tensor", zdone[min(j + qmax, NT) - 1])
            if j < 2:
                p.wait("sync", zdone[NT - 1])
            else:
                p.wait("sync", stev[j - 2])
            evx = p.dma("sync", xt[:, (j % 2) * DM:(j % 2 + 1) * DM],
                        xb[j * P:(j + 1) * P, :], "xr")
            for cb in range(2):
                k = 2 * j + cb
                if k >= 4:
                    p.wait("tensor", evacev[k - 4])
                for q in range(qmax):
                    ev = p.emit("tensor", lambda t, k=k, q=q, j=j, cb=cb, qm=qmax:
                                t.matmul(pys[k % 4][:, :], f2[:, q * P:(q + 1) * P],
                                         z0t[:, (j + q) * DM + cb * 512:
                                             (j + q) * DM + (cb + 1) * 512],
                                         start=(q == 0), stop=(q == qm - 1)),
                                sig="cmm" if q == qmax - 1 else None)
                mmev[k] = ev
            p.wait("vector", evx)
            if j >= 2:
                p.wait("vector", stev[j - 2])
            for cb in range(2):
                k = 2 * j + cb
                p.wait("vector", mmev[k])
                ev = p.emit("vector", lambda v, j=j, cb=cb, k=k: v.tensor_add(
                    osb[:, (j % 2) * DM + cb * 512:(j % 2) * DM + (cb + 1) * 512],
                    pys[k % 4][:, :],
                    xt[:, (j % 2) * DM + cb * 512:(j % 2) * DM + (cb + 1) * 512]),
                    sig="cev")
                evacev[k] = ev
            p.wait("gpsimd", ev)
            evs = p.dma("gpsimd", x1[j * P:(j + 1) * P, :],
                        osb[:, (j % 2) * DM:(j % 2 + 1) * DM], "cst")
            stev.append(evs)
        p.wait("gpsimd", p.cur("cst"))
        blk = st.enter_context(nc.Block())
        p.build(blk)
    return nc


def build_mpre():
    nc = bass.Bass("TRN2", target_bir_lowering=False, debug=False)
    win = nc.dram_tensor("win", [4096, DM], F32, kind="ExternalInput").ap()
    wout = nc.dram_tensor("wout", [DM, 4096], F32, kind="ExternalInput").ap()
    ident_d = nc.dram_tensor("ident", [P, P], F32, kind="ExternalInput").ap()
    mt = nc.dram_tensor("mt", [DM, DM], F32, kind="ExternalOutput").ap()
    woutT = nc.dram_tensor("woutT", [4096, DM], F32, kind="Internal").ap()
    with contextlib.ExitStack() as st:
        sb = lambda n, s, d=F32: st.enter_context(nc.sbuf_tensor(n, s, d))
        ident = sb("ident_s", [P, P])
        wo = sb("wo", [P, 2 * 4096])
        tst = sb("tst", [P, 4096])
        wi = sb("wi", [P, 2 * DM])
        wt = sb("wt", [P, 2 * 512])
        acc = sb("acc", [P, 8 * 512])
        pys = [st.enter_context(nc.psum_tensor(f"py{i}", [P, 512], F32))
               for i in range(8)]
        ptr = [pys[i][:, 0:P] for i in range(2)]
        p = Prog(nc, st)

        evi = p.dma("sync", ident[:, :], ident_d, "mld")
        p.wait("tensor", evi)
        stev = []
        for rb in range(8):
            if rb >= 2:
                p.wait("sync", stev[rb - 2])
            evl = p.dma("sync", wo[:, (rb % 2) * 4096:(rb % 2 + 1) * 4096],
                        wout[rb * P:(rb + 1) * P, :], "mwo")
            p.wait("tensor", evl)
            if rb >= 1:
                p.wait("tensor", stev[rb - 1])
            for fc in range(32):
                k = rb * 32 + fc
                if k >= 2:
                    p.wait("tensor", mevac[k - 2])
                ev = p.emit("tensor", lambda t, rb=rb, fc=fc, k=k: t.transpose(
                    ptr[k % 2][:, :],
                    wo[:, (rb % 2) * 4096 + fc * P:(rb % 2) * 4096 + (fc + 1) * P],
                    ident[:, :]), sig="mtr")
                p.wait("vector", ev)
                ev2 = p.emit("vector", lambda v, k=k, fc=fc: v.tensor_copy(
                    tst[:, fc * P:(fc + 1) * P], ptr[k % 2][:, :]), sig="mev")
                if k == 0:
                    mevac = []
                mevac.append(ev2)
            p.wait("gpsimd", ev2)
            ev3 = p.dma("gpsimd",
                        bass.AP(woutT.tensor, rb * P,
                                [[DM, 128], [P * DM, 32], [1, 128]]),
                        tst[:, :].rearrange("p (fc j) -> p fc j", j=P),
                        "mwt")
            stev.append(ev3)
        p.wait("sync", p.cur("mwt"))
        consumed = []
        for ob in range(2):
            if ob == 1:
                p.wait("tensor", p.cur("mac"))
            for fc in range(32):
                k = ob * 32 + fc
                if k >= 2:
                    p.wait("sync", consumed[k - 2])
                e1 = p.dma("sync", wi[:, (k % 2) * DM:(k % 2 + 1) * DM],
                           win[fc * P:(fc + 1) * P, :], "mld2")
                e2 = p.dma("sync", wt[:, (k % 2) * 512:(k % 2 + 1) * 512],
                           woutT[fc * P:(fc + 1) * P, ob * 512:(ob + 1) * 512], "mld2")
                p.wait("tensor", e2)
                for dc in range(8):
                    ev = p.emit("tensor", lambda t, k=k, dc=dc, fc=fc: t.matmul(
                        pys[dc][:, :],
                        wi[:, (k % 2) * DM + dc * P:(k % 2) * DM + (dc + 1) * P],
                        wt[:, (k % 2) * 512:(k % 2 + 1) * 512],
                        start=(fc == 0), stop=(fc == 31)), sig="mmm")
                consumed.append(ev)
            for dc in range(8):
                p.wait("vector", p.cur("mmm"))
                ev = p.emit("vector", lambda v, dc=dc: v.tensor_copy(
                    acc[:, dc * 512:(dc + 1) * 512], pys[dc][:, :]), sig="mac")
                p.wait("gpsimd", ev)
                p.dma("gpsimd", mt[dc * P:(dc + 1) * P, ob * 512:(ob + 1) * 512],
                      acc[:, dc * 512:(dc + 1) * 512], "mst")
        p.wait("gpsimd", p.cur("mst"))
        blk = st.enter_context(nc.Block())
        p.build(blk)
    return nc


def build_moe0():
    nc = bass.Bass("TRN2", target_bir_lowering=False, debug=False)
    x1 = nc.dram_tensor("x1", [T, DM], F32, kind="ExternalInput").ap()
    mts = nc.dram_tensor("mts", [8, DM, DM], F32, kind="ExternalInput").ap()
    wgate = nc.dram_tensor("wgate", [8, DM], F32, kind="ExternalInput").ap()
    ones1_d = nc.dram_tensor("ones1", [1, P], F32, kind="ExternalInput").ap()
    onescol_d = nc.dram_tensor("onescol", [P, 1], F32, kind="ExternalInput").ap()
    ident_d = nc.dram_tensor("ident", [P, P], F32, kind="ExternalInput").ap()
    x2f_o = nc.dram_tensor("x2f", [DM, T], F32, kind="ExternalOutput").ap()
    z2f_o = nc.dram_tensor("z2f", [DM, T], F32, kind="ExternalOutput").ap()
    grow_d = nc.dram_tensor("grow_d", [8, T], F32, kind="Internal").ap()
    NT = 16
    with contextlib.ExitStack() as st:
        sb = lambda n, s, d=F32: st.enter_context(nc.sbuf_tensor(n, s, d))
        x1f = sb("x1f", [P, 8 * T])
        z1f = sb("z1f", [P, 8 * T])
        x2f = x1f
        xt = sb("xt", [P, DM])
        zt = sb("zt", [P, DM])
        sq2 = sb("sq2", [P, 2 * 512])
        sqs = sq2
        stats = sb("stats", [P, 16])
        ones1 = sb("ones1_s", [1, P])
        onescol = sb("onescol_s", [P, 1])
        ident = sb("ident_s", [P, P])
        wgp = sb("wgp", [P, DM])
        wgT = sb("wgT", [P, 64])
        lt = sb("lt", [P, NT * 8])
        gt = sb("gt", [P, NT * 8])
        gsc = sb("gsc", [P, 40])
        grow_tb = sb("grow_tb", [1, 8 * 512])
        gsb = sb("gsb", [P, 8 * 512])
        zesb = sb("zesb", [P, 8 * 512])
        mtsb = sb("mtsb", [P, 2 * DM])
        lnst = sb("lnst", [1, 1024])
        pys = [st.enter_context(nc.psum_tensor(f"py{i}", [P, 512], F32))
               for i in range(8)]
        ptr = [pys[i][:, 0:P] for i in range(4)]
        p = Prog(nc, st)

        p.dma("sync", ones1[:, :], ones1_d, "mi")
        p.dma("sync", onescol[:, :], onescol_d, "mi")
        p.dma("sync", ident[:, :], ident_d, "mi")
        evms = p.emit("vector", lambda v: v.memset(wgp[:, :], 0.0), sig="mi")
        p.wait("sync", evms)
        p.dma("sync", wgp[0:8, :], wgate, "mi")
        p.wait("tensor", p.cur("mi"))

        trev = []
        for j in range(NT):
            buf = xt[:, :]
            zbuf = zt[:, :]
            if j >= 1:
                p.wait("sync", trev[j - 1])
            evd = p.dma("sync", buf, x1[j * P:(j + 1) * P, :], "xld")
            p.wait("vector", evd)
            evz = ln_token_tile(p, buf, zbuf, sqs[:, 0:DM], stats)
            p.wait("tensor", evd)
            for dc in range(8):
                k = j * 16 + dc
                if k >= 4:
                    p.wait("tensor", trev_e[k - 4])
                ev = p.emit("tensor", lambda t, dc=dc, k=k: t.transpose(
                    ptr[k % 4][:, :], xt[:, dc * P:(dc + 1) * P],
                    ident[:, :]), sig="ttr")
                p.wait("vector", ev)
                ev2 = p.emit("vector", lambda v, j=j, dc=dc, k=k: v.tensor_copy(
                    x1f[:, dc * T + j * P:dc * T + (j + 1) * P], ptr[k % 4][:, :]),
                    sig="tev")
                if k == 0:
                    trev_e = []
                trev_e.append(ev2)
            p.wait("tensor", evz)
            for dc in range(8):
                k = j * 16 + 8 + dc
                p.wait("tensor", trev_e[k - 4])
                ev = p.emit("tensor", lambda t, dc=dc, k=k: t.transpose(
                    ptr[k % 4][:, :], zt[:, dc * P:(dc + 1) * P],
                    ident[:, :]), sig="ttr")
                p.wait("vector", ev)
                ev2 = p.emit("vector", lambda v, j=j, dc=dc, k=k: v.tensor_copy(
                    z1f[:, dc * T + j * P:dc * T + (j + 1) * P], ptr[k % 4][:, :]),
                    sig="tev")
                trev_e.append(ev2)
            trev.append(ev2)

        for k in range(8):
            p.wait("tensor", trev_e[16 * NT - 4 + k] if k < 4 else wg_e[k - 4])
            ev = p.emit("tensor", lambda t, k=k: t.transpose(
                ptr[k % 4][:, :], wgp[:, k * P:(k + 1) * P], ident[:, :]), sig="ttr")
            p.wait("vector", ev)
            ev2 = p.emit("vector", lambda v, k=k: v.tensor_copy(
                wgT[:, k * 8:(k + 1) * 8], ptr[k % 4][:, 0:8]), sig="tev")
            if k == 0:
                wg_e = []
            wg_e.append(ev2)

        p.wait("tensor", wg_e[7])
        lt_ev = []
        for j in range(NT):
            for dc in range(8):
                ev = p.emit("tensor", lambda t, j=j, dc=dc: t.matmul(
                    pys[4][:, 0:8], z1f[:, dc * T + j * P:dc * T + (j + 1) * P],
                    wgT[:, dc * 8:(dc + 1) * 8],
                    start=(dc == 0), stop=(dc == 7)), sig="lmm")
            p.wait("vector", ev)
            ev2 = p.vemit(lambda v, j=j: v.tensor_copy(
                lt[:, j * 8:(j + 1) * 8], pys[4][:, 0:8]), sig="lev")
            lt_ev.append(ev2)
            p.wait("tensor", ev2)

        gdone = []
        for j in range(NT):
            l8 = lt[:, j * 8:(j + 1) * 8]
            g8 = gt[:, j * 8:(j + 1) * 8]
            m8 = gsc[:, 0:8]
            eq = gsc[:, 8:16]
            fnd = gsc[:, 16:17]
            sel1 = gsc[:, 17:25]
            lm = gsc[:, 25:33]
            nf = gsc[:, 33:34]
            p.vemit(lambda v, l8=l8, m8=m8: v.max(m8, l8))
            p.vemit(lambda v, l8=l8, eq=eq, m8=m8: v.tensor_scalar(
                eq, l8, m8[:, 0:1], None, OP.is_equal))
            p.vemit(lambda v, fnd=fnd: v.memset(fnd, 0.0))
            for e in range(8):
                p.vemit(lambda v, fnd=fnd, nf=nf: v.tensor_scalar(
                    nf, fnd, -1.0, 1.0, OP.mult, OP.add))
                p.vemit(lambda v, sel1=sel1, eq=eq, nf=nf, e=e: v.tensor_mul(
                    sel1[:, e:e + 1], eq[:, e:e + 1], nf))
                p.vemit(lambda v, fnd=fnd, eq=eq, e=e: v.tensor_max(
                    fnd, fnd, eq[:, e:e + 1]))
            p.vemit(lambda v, lm=lm, sel1=sel1, l8=l8: v.scalar_tensor_tensor(
                lm, sel1, -1e30, l8, OP.mult, OP.add))
            p.vemit(lambda v, eq=eq, lm=lm, m8=m8: v.tensor_scalar(
                eq, lm, m8[:, 1:2], None, OP.is_equal))
            p.vemit(lambda v, fnd=fnd: v.memset(fnd, 0.0))
            for e in range(8):
                p.vemit(lambda v, fnd=fnd, nf=nf: v.tensor_scalar(
                    nf, fnd, -1.0, 1.0, OP.mult, OP.add))
                p.vemit(lambda v, eq=eq, nf=nf, e=e: v.tensor_tensor(
                    eq[:, e:e + 1], eq[:, e:e + 1], nf, OP.mult))
                p.vemit(lambda v, fnd=fnd, eq=eq, e=e: v.tensor_max(
                    fnd, fnd, eq[:, e:e + 1]))
            ev = p.emit("vector", lambda v, m8=m8: v.tensor_sub(
                stats[:, 7:8], m8[:, 1:2], m8[:, 0:1]), sig="gv")
            p.wait("scalar", ev)
            evs = p.emit("scalar", lambda s: s.activation(
                stats[:, 8:9], stats[:, 7:8], AF.Exp), sig="gs")
            p.wait("vector", evs)
            p.vemit(lambda v: v.tensor_scalar(stats[:, 9:10], stats[:, 8:9],
                                              1.0, 1.0, OP.mult, OP.add))
            p.vemit(lambda v: v.reciprocal(stats[:, 10:11], stats[:, 9:10]))
            p.vemit(lambda v: v.tensor_mul(stats[:, 11:12], stats[:, 8:9],
                                           stats[:, 10:11]))
            p.vemit(lambda v, g8=g8, sel1=sel1: v.tensor_scalar(
                g8, sel1, stats[:, 10:11], None, OP.mult))
            ev = p.emit("vector", lambda v, g8=g8, eq=eq: v.scalar_tensor_tensor(
                g8, eq, stats[:, 11:12], g8, OP.mult, OP.add), sig="gz")
            gdone.append(ev)

        gr_ev = []
        for j in range(NT):
            p.wait("tensor", gdone[j])
            for e in range(8):
                k = j * 8 + e
                if k >= 4:
                    p.wait("tensor", gr_ev[k - 4])
                ev = p.emit("tensor", lambda t, j=j, e=e, k=k: t.transpose(
                    ptr[k % 4][0:1, :], gt[:, j * 8 + e:j * 8 + e + 1], ident[:, :]),
                    sig="ttr")
                p.wait("vector", ev)
                ev2 = p.emit("vector", lambda v, j=j, e=e, k=k: v.tensor_copy(
                    gsb[0:1, e * 512 + (j % 4) * P:e * 512 + (j % 4 + 1) * P],
                    ptr[k % 4][0:1, :]), sig="grw")
                gr_ev.append(ev2)
            if j % 4 == 3:
                p.wait("gpsimd", ev2)
                evst = p.dma("gpsimd",
                             bass.AP(grow_d.tensor, (j - 3) * P, [[1, 1], [T, 8], [1, 512]]),
                             gsb[0:1, :].rearrange("p (e u) -> p e u", u=512), "gst")
                if j < NT - 1:
                    p.wait("vector", evst)

        p.wait("tensor", p.cur("grw"))
        p.wait("sync", p.cur("gst"))
        mt_cons = []
        x2_ev = []
        zconsL = []
        n = 0
        for tb in range(4):
            evg = p.dma("sync",
                        grow_tb[0:1, :].rearrange("p (e u) -> p e u", u=512),
                        bass.AP(grow_d.tensor, tb * 512, [[1, 1], [T, 8], [1, 512]]),
                        "gld")
            p.wait("tensor", evg)
            gb_ev = []
            for e in range(8):
                if tb > 0 and e == 0:
                    for ee in x2_ev[-8:]:
                        p.wait("tensor", ee)
                ev = p.emit("tensor", lambda t, e=e: t.matmul(
                    pys[e][:, :], ones1[:, :], grow_tb[0:1, e * 512:(e + 1) * 512],
                    start=True, stop=True), sig="gbm")
                p.wait("vector", ev)
                ev2 = p.emit("vector", lambda v, e=e: v.tensor_copy(
                    gsb[:, e * 512:(e + 1) * 512], pys[e][:, :]), sig="gbe")
                gb_ev.append(ev2)
            for e in range(8):
                idx = tb * 8 + e
                if idx >= 1:
                    p.wait("vector", zconsL[idx - 1])
                p.wait("vector", gb_ev[e])
                for dc in range(8):
                    ev = p.emit("vector", lambda v, tb=tb, e=e, dc=dc:
                                v.tensor_mul(
                                    zesb[:, dc * 512:(dc + 1) * 512],
                                    z1f[:, dc * T + tb * 512:dc * T + (tb + 1) * 512],
                                    gsb[:, e * 512:(e + 1) * 512]), sig="zer")
                p.wait("tensor", ev)
                if e == 0:
                    p.wait("tensor", gb_ev[7])
                for dc in range(8):
                    n += 1
                    if n > 2:
                        p.wait("sync", mt_cons[n - 3])
                    evm = p.dma("sync", mtsb[:, (n % 2) * DM:(n % 2 + 1) * DM],
                                mts[e, dc * P:(dc + 1) * P, :], "mtl")
                    p.wait("tensor", evm)
                    for oc in range(8):
                        ev = p.emit("tensor", lambda t, n=n, dc=dc, oc=oc, e=e:
                                    t.matmul(
                                        pys[oc][:, :],
                                        mtsb[:, (n % 2) * DM + oc * P:(n % 2) * DM + (oc + 1) * P],
                                        zesb[:, dc * 512:(dc + 1) * 512],
                                        start=(e == 0 and dc == 0),
                                        stop=(e == 7 and dc == 7)), sig="ymm")
                    mt_cons.append(ev)
                zconsL.append(ev)
            for oc in range(8):
                p.wait("vector", p.cur("ymm"))
                ev = p.emit("vector", lambda v, tb=tb, oc=oc: v.tensor_add(
                    x2f[:, oc * T + tb * 512:oc * T + (tb + 1) * 512], pys[oc][:, :],
                    x1f[:, oc * T + tb * 512:oc * T + (tb + 1) * 512]), sig="x2a")
                x2_ev.append(ev)

        z2f = z1f
        for ee in x2_ev[-8:]:
            p.wait("tensor", ee)
        p.wait("scalar", x2_ev[-1])
        for tb in range(4):
            xs = [x2f[:, dc * T + tb * 512:dc * T + (tb + 1) * 512] for dc in range(8)]
            zs = [z2f[:, dc * T + tb * 512:dc * T + (tb + 1) * 512] for dc in range(8)]
            evs = ln_feat_tb(p, pys, xs, zs, onescol, ones1, sq2, lnst)
        p.wait("gpsimd", p.cur("lf_z"))
        p.wait("gpsimd", p.cur("x2a"))
        for dc in range(8):
            p.dma("gpsimd", x2f_o[dc * P:(dc + 1) * P, :], x2f[:, dc * T:(dc + 1) * T], "ost")
            p.dma("gpsimd", z2f_o[dc * P:(dc + 1) * P, :], z2f[:, dc * T:(dc + 1) * T], "ost")
        p.wait("gpsimd", p.cur("ost"))
        blk = st.enter_context(nc.Block())
        p.build(blk)
    return nc


def build_attn():
    nc = bass.Bass("TRN2", target_bir_lowering=False, debug=False)
    z2all = nc.dram_tensor("z2all", [8, DM, T], F32, kind="ExternalInput").ap()
    wq = nc.dram_tensor("wq", [DM, P], F32, kind="ExternalInput").ap()
    wk = nc.dram_tensor("wk", [DM, P], F32, kind="ExternalInput").ap()
    wv = nc.dram_tensor("wv", [DM, P], F32, kind="ExternalInput").ap()
    ext3 = nc.dram_tensor("ext3", [P, 4608], F32, kind="ExternalInput").ap()
    ao = nc.dram_tensor("ao", [4, 2, 8, 4, P, 64], F32, kind="ExternalOutput").ap()
    ident_d = nc.dram_tensor("ident", [P, P], F32, kind="ExternalInput").ap()
    ones1_d = nc.dram_tensor("ones1", [1, P], F32, kind="ExternalInput").ap()
    with contextlib.ExitStack() as st:
        sb = lambda n, s, d=F32: st.enter_context(nc.sbuf_tensor(n, s, d))
        ident = sb("ident_s", [P, P])
        ones1 = sb("ones1_s", [1, P])
        wqs = sb("wqs", [P, 8 * P])
        wks = sb("wks", [P, 8 * P])
        wvs = sb("wvs", [P, 8 * P])
        zsl = sb("zsl", [P, 4 * 512])
        qp = sb("qp", [P, S])
        kf = sb("kf", [P, S])
        vp = sb("vp", [P, 32 * 130])
        bias = sb("biasb", [P, 32 * 512])
        p32 = sb("p32", [P, 4 * 512])
        ep = sb("ep", [P, 2 * 512])
        rcp = sb("rcp", [P, 8])
        epb = sb("epb", [P, 2 * 64])
        osb = sb("osb", [P, 4 * 64])
        pys = [st.enter_context(nc.psum_tensor(f"py{i}", [P, 512], F32))
               for i in range(8)]
        p = Prog(nc, st)

        p.dma("sync", ident[:, :], ident_d, "mi")
        p.dma("sync", ones1[:, :], ones1_d, "mi")
        for dc in range(8):
            p.dma("sync", wqs[:, dc * P:(dc + 1) * P], wq[dc * P:(dc + 1) * P, :], "mi")
            p.dma("sync", wks[:, dc * P:(dc + 1) * P], wk[dc * P:(dc + 1) * P, :], "mi")
            p.dma("sync", wvs[:, dc * P:(dc + 1) * P], wv[dc * P:(dc + 1) * P, :], "mi")
        p.wait("tensor", p.cur("mi"))
        evones = p.emit("vector", lambda v: v.memset(
            vp[:, :].rearrange("p (tc c) -> p tc c", c=130)[:, :, 64:65], 1.0), sig="vo")
        evones2 = p.emit("vector", lambda v: v.memset(
            vp[:, :].rearrange("p (tc c) -> p tc c", c=130)[:, :, 129:130], 1.0), sig="vo")

        bias_free = None
        pvbv_free = None
        eov_free = None
        ost_last = None
        qkv_free = None
        for b in range(4):
            zev = []
            ndone = 0
            for r2 in range(2):
                r = 2 * b + r2
                for nb2 in range(4):
                    blkq = r2 * 4 + nb2
                    for dc in range(8):
                        k = (blkq * 8 + dc)
                        if k >= 4:
                            p.wait("sync", zcons2[k - 4])
                        evz = p.dma("sync", zsl[:, (k % 4) * 512:(k % 4 + 1) * 512],
                                    z2all[r, dc * P:(dc + 1) * P, nb2 * 512:(nb2 + 1) * 512],
                                    "zld")
                        p.wait("tensor", evz)
                        if k == 0:
                            zcons2 = []
                            if qkv_free is not None:
                                p.wait("tensor", qkv_free)
                            if eov_free is not None:
                                p.wait("tensor", eov_free)
                            p.wait("tensor", p.cur("vo"))
                        sl = zsl[:, (k % 4) * 512:(k % 4 + 1) * 512]
                        p.emit("tensor", lambda t, sl=sl, dc=dc: t.matmul(
                            pys[0][:, :], wqs[:, dc * P:(dc + 1) * P], sl,
                            start=(dc == 0), stop=(dc == 7)))
                        p.emit("tensor", lambda t, sl=sl, dc=dc: t.matmul(
                            pys[1][:, :], wks[:, dc * P:(dc + 1) * P], sl,
                            start=(dc == 0), stop=(dc == 7)))
                        for tc in range(4):
                            ev = p.emit("tensor", lambda t, sl=sl, dc=dc, tc=tc: t.matmul(
                                pys[2 + tc][:, 0:P],
                                sl[:, tc * P:(tc + 1) * P],
                                wvs[:, dc * P:(dc + 1) * P],
                                start=(dc == 0), stop=(dc == 7)), sig="qkv")
                        zcons2.append(ev)
                    p.wait("vector", ev)
                    off = 255 - 128 * r2 - 32 * nb2
                    qv_ = pys[0][:, :].rearrange("p (l1 l0) -> p l1 l0", l0=16)
                    qdst = qp[:, :].rearrange("p (l0 u) -> p l0 u", u=256)
                    for h in range(2):
                        src = qv_[64 * h:64 * h + 64].transpose([0, 2, 1])
                        dst = qdst[64 * h:64 * h + 64, :,
                                   off:(off - 32 if off - 32 >= 0 else None):-1]
                        ev = p.emit("vector", lambda v, src=src, dst=dst:
                                    v.tensor_copy(dst, src), sig="qev")
                        ev = p.emit("vector", lambda v, h=h, blkq=blkq: v.tensor_copy(
                            kf[64 * h:64 * h + 64, blkq * 512:(blkq + 1) * 512],
                            pys[1][64 * h:64 * h + 64, :]), sig="qev")
                    vdst = vp[:, :].rearrange("p (tc c) -> p tc c", c=130)
                    for tc in range(4):
                        vsrc = pys[2 + tc][:, 0:P].rearrange(
                            "p (hh d) -> p hh d", hh=2, d=64)
                        vd = vdst[:, blkq * 4 + tc:blkq * 4 + tc + 1, :] \
                            .rearrange("p tc (hh x) -> p tc hh x", hh=2, x=65)[:, 0, :, 0:64]
                        ev = p.emit("vector", lambda v, vd=vd, vsrc=vsrc:
                                    v.tensor_copy(vd, vsrc), sig="qev")
                    evq = ev
                    p.wait("tensor", evq)
            qkv_done = p.cur("qev")

            for h in range(2):
                for qv in range(8):
                    if bias_free is not None:
                        p.wait("sync", bias_free)
                    bev = []
                    for kg in range(4):
                        srcap = bass.AP(ext3.tensor,
                                        256 + 512 * qv - 256 * h + 64 * kg,
                                        [[4608, 128], [8, 8], [1, 512]])
                        dstap = bias[:, kg * 8 * 512:(kg + 1) * 8 * 512] \
                            .rearrange("p (kc u) -> p kc u", u=512)
                        ev = p.dma("sync", dstap, srcap, "bld")
                        bev.append(ev)
                    for e_ in bev:
                        p.wait("tensor", e_)
                    p.wait("tensor", qkv_done)
                    if pvbv_free is not None:
                        p.wait("tensor", pvbv_free)
                    expv = []
                    pvdone = []
                    for kc in range(32):
                        if kc >= 2:
                            p.wait("tensor", expv[kc - 2])
                        ev = p.emit("tensor", lambda t, h=h, qv=qv, kc=kc: t.matmul(
                            pys[kc % 2][:, :],
                            kf[64 * h:64 * h + 64, kc * P:(kc + 1) * P],
                            qp[64 * h:64 * h + 64, qv * 512:(qv + 1) * 512],
                            start=True, stop=True), sig="scm")
                        p.wait("scalar", ev)
                        if kc >= 4:
                            p.wait("scalar", pvdone[kc - 4])
                        eva = p.emit("scalar", lambda s, kc=kc: s.activation(
                            p32[:, (kc % 4) * 512:(kc % 4 + 1) * 512],
                            pys[kc % 2][:, :],
                            AF.Exp, scale=0.03125), sig="expv")
                        expv.append(eva)
                        p.wait("tensor", eva)
                        ev1 = p.emit("tensor", lambda t, kc=kc, h=h: t.matmul(
                            pys[2][0:65, :],
                            vp[:, kc * 130 + 65 * h:kc * 130 + 65 * h + 65],
                            p32[:, (kc % 4) * 512:(kc % 4 + 1) * 512],
                            start=(kc == 0), stop=(kc == 31)), sig="pvm")
                        ev2 = p.emit("tensor", lambda t, kc=kc, h=h: t.matmul(
                            pys[3][0:64, :],
                            vp[:, kc * 130 + 65 * h:kc * 130 + 65 * h + 64],
                            bias[:, kc * 512:(kc + 1) * 512],
                            start=(kc == 0), stop=(kc == 31)), sig="pvm")
                        pvdone.append(ev1)
                    bias_free = ev2
                    p.wait("vector", ev1)
                    p.wait("vector", ev2)
                    if ost_last is not None:
                        p.wait("vector", ost_last)
                    e1 = p.emit("vector", lambda v: v.tensor_copy(ep[0:65, 0:512],
                                                                  pys[2][0:65, :]), sig="epv")
                    e2 = p.emit("vector", lambda v: v.tensor_copy(ep[0:64, 512:1024],
                                                                  pys[3][0:64, :]), sig="epv")
                    pvbv_free = e2
                    p.wait("tensor", e1)
                    p.wait("tensor", e2)
                    if eov_free is not None:
                        p.wait("tensor", eov_free)
                    trevs = []
                    for kt in range(4):
                        if kt >= 2:
                            p.wait("tensor", trevs[kt - 2])
                        ev = p.emit("tensor", lambda t, kt=kt: t.transpose(
                            pys[4 + (kt % 2)][:, 0:65], ep[0:65, kt * P:(kt + 1) * P],
                            ident[0:65, 0:65]), sig="eptr")
                        ev2 = p.emit("tensor", lambda t, kt=kt: t.transpose(
                            pys[6 + (kt % 2)][:, 0:64], ep[0:64, 512 + kt * P:512 + (kt + 1) * P],
                            ident[0:64, 0:64]), sig="eptr")
                        p.wait("vector", ev2)
                        p.vemit(lambda v, kt=kt: v.reciprocal(
                            rcp[:, kt:kt + 1], pys[4 + (kt % 2)][:, 64:65]))
                        p.vemit(lambda v, kt=kt: v.tensor_copy(
                            epb[:, (kt % 2) * 64:(kt % 2 + 1) * 64],
                            pys[6 + (kt % 2)][:, 0:64]))
                        e3 = p.emit("vector", lambda v, kt=kt: v.scalar_tensor_tensor(
                            osb[:, kt * 64:(kt + 1) * 64],
                            pys[4 + (kt % 2)][:, 0:64], rcp[:, kt:kt + 1],
                            epb[:, (kt % 2) * 64:(kt % 2 + 1) * 64],
                            OP.mult, OP.add), sig="eov")
                        trevs.append(e3)
                        p.wait("gpsimd", e3)
                        ost_last = p.dma("gpsimd", ao[b, h, qv, kt, :, :],
                                         osb[:, kt * 64:(kt + 1) * 64], "ost")
                    eov_free = trevs[-1]
            qkv_free = pvbv_free
        p.wait("gpsimd", p.cur("ost"))
        blk = st.enter_context(nc.Block())
        p.build(blk)
    return nc


def _vslice(vdst, blkq):
    v = vdst[:, blkq * 4:(blkq + 1) * 4, :]
    return v.rearrange("p tc (hh x) -> p tc hh x", hh=2, x=65)[:, :, :, 0:64]


def build_ffn1(stages=3):
    nc = bass.Bass("TRN2", target_bir_lowering=False, debug=False)
    x2f_i = nc.dram_tensor("x2f", [DM, T], F32, kind="ExternalInput").ap()
    aof_i = nc.dram_tensor("aof", [DM, T], F32, kind="ExternalInput").ap()
    gm_i = nc.dram_tensor("gm", [8, P], F32, kind="ExternalInput").ap()
    bt_i = nc.dram_tensor("bt", [8, P], F32, kind="ExternalInput").ap()
    w1_i = nc.dram_tensor("w1", [4, DM, 4096], F32, kind="ExternalInput").ap()
    b1_i = nc.dram_tensor("b1", [4, 32, P], F32, kind="ExternalInput").ap()
    w2_i = nc.dram_tensor("w2", [4, 4096, DM], F32, kind="ExternalInput").ap()
    b2_i = nc.dram_tensor("b2", [4, 8, P], F32, kind="ExternalInput").ap()
    wg_i = nc.dram_tensor("wg", [DM, 4], F32, kind="ExternalInput").ap()
    bg_i = nc.dram_tensor("bg", [1, 4], F32, kind="ExternalInput").ap()
    ones1_d = nc.dram_tensor("ones1", [1, P], F32, kind="ExternalInput").ap()
    onescol_d = nc.dram_tensor("onescol", [P, 1], F32, kind="ExternalInput").ap()
    ident_d = nc.dram_tensor("ident", [P, P], F32, kind="ExternalInput").ap()
    xo = nc.dram_tensor("xo", [T, DM], F32, kind="ExternalOutput").ap()
    x3sc = nc.dram_tensor("x3sc", [DM, T], F32, kind="Internal").ap()
    mrow_d = nc.dram_tensor("mrow_d", [4, T], F32, kind="Internal").ap()
    NT = 16
    with contextlib.ExitStack() as st:
        sb = lambda n, s, d=F32: st.enter_context(nc.sbuf_tensor(n, s, d))
        A = sb("A", [P, 8 * T])
        Bb = sb("Bb", [P, 8 * T])
        C = sb("C", [P, 8 * T])
        ident = sb("ident_s", [P, P])
        ones1 = sb("ones1_s", [1, P])
        onescol = sb("onescol_s", [P, 1])
        sq2 = sb("sq2", [P, 2 * 512])
        lnst = sb("lnst", [1, 1024])
        tpad = sb("tpad", [P, P])
        gbt = sb("gbt", [P, 2 * 8])
        b1t = sb("b1t", [P, 4 * 32])
        b2t = sb("b2t", [P, 4 * 8])
        bgr = sb("bgr", [P, 4])
        wgsb = sb("wgsb", [P, 32])
        lt = sb("lt", [P, NT * 4])
        gsc = sb("gsc", [P, 24])

        pys = [st.enter_context(nc.psum_tensor(f"py{i}", [P, 512], F32))
               for i in range(8)]
        ptr = [pys[4 + i][:, 0:P] for i in range(4)]
        p = Prog(nc, st)

        p.dma("sync", ident[:, :], ident_d, "mi")
        p.dma("sync", ones1[:, :], ones1_d, "mi")
        p.dma("sync", onescol[:, :], onescol_d, "mi")
        for dc in range(8):
            p.dma("sync", A[:, dc * T:(dc + 1) * T], x2f_i[dc * P:(dc + 1) * P, :], "mi")
            p.dma("sync", Bb[:, dc * T:(dc + 1) * T], aof_i[dc * P:(dc + 1) * P, :], "mi")
        if stages == 0:
            osb = Bb[:, 8 * T - 2 * DM:8 * T - DM]
            osb2 = Bb[:, 8 * T - DM:8 * T]
            p.wait("tensor", p.cur("mi"))
            stv = []
            fe_ev = []
            for j in range(NT):
                ob = [osb, osb2][j % 2]
                if j >= 2:
                    p.wait("vector", stv[j - 2])
                for dc in range(8):
                    k = j * 8 + dc
                    if k >= 4:
                        p.wait("tensor", fe_ev[k - 4])
                    ev = p.emit("tensor", lambda t, j=j, dc=dc, k=k: t.transpose(
                        ptr[k % 4][:, :], A[:, dc * T + j * P:dc * T + (j + 1) * P],
                        ident[:, :]), sig="ftr")
                    p.wait("vector", ev)
                    ev2 = p.emit("vector", lambda v, dc=dc, ob=ob, k=k: v.tensor_copy(
                        ob[:, dc * P:(dc + 1) * P], ptr[k % 4][:, :]), sig="fev")
                    fe_ev.append(ev2)
                p.wait("gpsimd", ev2)
                evs_ = p.dma("gpsimd", xo[j * P:(j + 1) * P, :], ob, "fst")
                stv.append(evs_)
            p.wait("gpsimd", p.cur("fst"))
            blk = st.enter_context(nc.Block())
            p.build(blk)
            return nc

        tload_state = {"prev": None}

        def tload(dsts, srcs, width):
            for d_, s_ in zip(dsts, srcs):
                if tload_state["prev"] is not None:
                    p.wait("vector", tload_state["prev"])
                ev0 = p.emit("vector", lambda v: v.memset(tpad[:, :], 0.0), sig="tm")
                p.wait("sync", ev0)
                ev1 = p.dma("sync", tpad[0:s_.shape[0], :], s_, "tm2")
                p.wait("tensor", ev1)
                ev = p.emit("tensor", lambda t: t.transpose(ptr[0][:, :], tpad[:, :],
                                                            ident[:, :]), sig="ttr")
                p.wait("vector", ev)
                ev2 = p.emit("vector", lambda v, d_=d_, w=width: v.tensor_copy(
                    d_, ptr[0][:, 0:w]), sig="tld")
                tload_state["prev"] = ev2
        p.wait("tensor", p.cur("mi"))
        tload([gbt[:, 0:8]], [gm_i], 8)
        tload([gbt[:, 8:16]], [bt_i], 8)
        for e in range(4):
            tload([b1t[:, e * 32:(e + 1) * 32]], [b1_i[e]], 32)
            tload([b2t[:, e * 8:(e + 1) * 8]], [b2_i[e]], 8)
        for dc in range(8):
            p.dma("sync", wgsb[:, dc * 4:(dc + 1) * 4], wg_i[dc * P:(dc + 1) * P, :], "mi")
        if tload_state["prev"] is not None:
            p.wait("vector", tload_state["prev"])
        ev0 = p.emit("vector", lambda v: v.memset(tpad[0:1, 0:4], 0.0), sig="tm")
        p.wait("sync", ev0)
        evbg = p.dma("sync", tpad[0:1, 0:4], bg_i, "tm2")
        p.wait("tensor", evbg)
        ev = p.emit("tensor", lambda t: t.matmul(pys[0][:, 0:4], ones1[:, :],
                                                 tpad[0:1, 0:4], start=True, stop=True),
                    sig="ttr")
        p.wait("vector", ev)
        p.emit("vector", lambda v: v.tensor_copy(bgr[:, :], pys[0][:, 0:4]), sig="tld")

        p.wait("vector", p.cur("mi"))
        p.wait("tensor", p.cur("mi"))
        p.wait("tensor", p.cur("tld"))
        p.wait("scalar", p.cur("tld"))

        gam = [gbt[:, dc:dc + 1] for dc in range(8)]
        bet = [gbt[:, 8 + dc:8 + dc + 1] for dc in range(8)]
        for tb in range(4):
            aos = [Bb[:, dc * T + tb * 512:dc * T + (tb + 1) * 512] for dc in range(8)]
            zs = [C[:, dc * T + tb * 512:dc * T + (tb + 1) * 512] for dc in range(8)]
            evs = ln_feat_tb(p, pys, aos, zs, onescol, ones1, sq2, lnst,
                             gam=gam, bet=bet, eps=EPS_ATTN)
            x3evs = []
            for dc in range(8):
                p.wait("vector", evs[dc])
                ev = p.emit("vector", lambda v, dc=dc, tb=tb: v.tensor_add(
                    A[:, dc * T + tb * 512:dc * T + (tb + 1) * 512],
                    A[:, dc * T + tb * 512:dc * T + (tb + 1) * 512],
                    C[:, dc * T + tb * 512:dc * T + (tb + 1) * 512]), sig="x3a")
                x3evs.append(ev)
            for ev in x3evs:
                p.wait("tensor", ev)
                p.wait("scalar", ev)
            x3s = [A[:, dc * T + tb * 512:dc * T + (tb + 1) * 512] for dc in range(8)]
            evs2 = ln_feat_tb(p, pys, x3s, zs, onescol, ones1, sq2, lnst)
            for ev in evs2:
                p.wait("gpsimd", ev)
            for dc in range(8):
                p.dma("gpsimd", x3sc[dc * P:(dc + 1) * P, tb * 512:(tb + 1) * 512],
                      A[:, dc * T + tb * 512:dc * T + (tb + 1) * 512], "x3st")

        if stages == 1:
            osb = Bb[:, 8 * T - 2 * DM:8 * T - DM]
            osb2 = Bb[:, 8 * T - DM:8 * T]
            p.wait("tensor", p.cur("x3a"))
            stv = []
            fe_ev = []
            for j in range(NT):
                ob = [osb, osb2][j % 2]
                if j >= 2:
                    p.wait("vector", stv[j - 2])
                for dc in range(8):
                    k = j * 8 + dc
                    if k >= 4:
                        p.wait("tensor", fe_ev[k - 4])
                    ev = p.emit("tensor", lambda t, j=j, dc=dc, k=k: t.transpose(
                        ptr[k % 4][:, :], A[:, dc * T + j * P:dc * T + (j + 1) * P],
                        ident[:, :]), sig="ftr")
                    p.wait("vector", ev)
                    ev2 = p.emit("vector", lambda v, dc=dc, ob=ob, k=k: v.tensor_copy(
                        ob[:, dc * P:(dc + 1) * P], ptr[k % 4][:, :]), sig="fev")
                    fe_ev.append(ev2)
                p.wait("gpsimd", ev2)
                evs_ = p.dma("gpsimd", xo[j * P:(j + 1) * P, :], ob, "fst")
                stv.append(evs_)
            p.wait("gpsimd", p.cur("fst"))
            blk = st.enter_context(nc.Block())
            p.build(blk)
            return nc

        lt_ev = []
        p.wait("tensor", p.cur("lf_z"))
        for j in range(NT):
            for dc in range(8):
                ev = p.emit("tensor", lambda t, j=j, dc=dc: t.matmul(
                    pys[0][:, 0:4], C[:, dc * T + j * P:dc * T + (j + 1) * P],
                    wgsb[:, dc * 4:(dc + 1) * 4],
                    start=(dc == 0), stop=(dc == 7)), sig="lmm")
            p.wait("vector", ev)
            ev2 = p.vemit(lambda v, j=j: v.tensor_add(
                lt[:, j * 4:(j + 1) * 4], pys[0][:, 0:4], bgr[:, :]), sig="lev")
            lt_ev.append(ev2)
            p.wait("tensor", ev2)
        ohev = []
        for j in range(NT):
            l4 = lt[:, j * 4:(j + 1) * 4]
            m1 = gsc[:, 0:1]
            eq = gsc[:, 1:5]
            fnd = gsc[:, 5:6]
            nf = gsc[:, 6:7]
            p.vemit(lambda v, l4=l4, m1=m1: v.reduce_max(m1, l4, axis=AX))
            p.vemit(lambda v, l4=l4, eq=eq, m1=m1: v.tensor_scalar(
                eq, l4, m1, None, OP.is_equal))
            p.vemit(lambda v, fnd=fnd: v.memset(fnd, 0.0))
            for e in range(4):
                p.vemit(lambda v, fnd=fnd, nf=nf: v.tensor_scalar(
                    nf, fnd, -1.0, 1.0, OP.mult, OP.add))
                p.vemit(lambda v, eq=eq, nf=nf, e=e: v.tensor_mul(
                    eq[:, e:e + 1], eq[:, e:e + 1], nf))
                p.vemit(lambda v, fnd=fnd, eq=eq, e=e: v.tensor_max(
                    fnd, fnd, eq[:, e:e + 1]))
            ev = p.emit("vector", lambda v, l4=l4, eq=eq: v.tensor_copy(l4, eq), sig="ohv")
            ohev.append(ev)
        mr_ev = []
        for j in range(NT):
            p.wait("tensor", ohev[j])
            for e in range(4):
                k = j * 4 + e
                if k >= 2:
                    p.wait("tensor", mr_ev[k - 2])
                ev = p.emit("tensor", lambda t, j=j, e=e, k=k: t.transpose(
                    ptr[k % 4][0:1, :], lt[:, j * 4 + e:j * 4 + e + 1], ident[:, :]),
                    sig="ttr")
                p.wait("vector", ev)
                if k >= 2:
                    p.wait("vector", mst_ev[k - 2])
                ev2 = p.emit("vector", lambda v, k=k: v.tensor_copy(
                    lnst[0:1, (k % 2) * P:(k % 2 + 1) * P], ptr[k % 4][0:1, :]),
                    sig="mrw")
                mr_ev.append(ev2)
                p.wait("gpsimd", ev2)
                ev3 = p.dma("gpsimd", mrow_d[e, j * P:(j + 1) * P],
                            lnst[0:1, (k % 2) * P:(k % 2 + 1) * P], "mst")
                if k == 0:
                    mst_ev = []
                mst_ev.append(ev3)
        p.wait("sync", p.cur("mst"))

        if stages == 2:
            osb = Bb[:, 8 * T - 2 * DM:8 * T - DM]
            osb2 = Bb[:, 8 * T - DM:8 * T]
            p.wait("tensor", p.cur("mrw"))
            p.wait("sync", p.cur("mst"))
            p.wait("tensor", p.cur("x3a"))
            stv = []
            fe_ev = []
            for j in range(NT):
                ob = [osb, osb2][j % 2]
                if j >= 2:
                    p.wait("vector", stv[j - 2])
                for dc in range(8):
                    k = j * 8 + dc
                    if k >= 4:
                        p.wait("tensor", fe_ev[k - 4])
                    ev = p.emit("tensor", lambda t, j=j, dc=dc, k=k: t.transpose(
                        ptr[k % 4][:, :], A[:, dc * T + j * P:dc * T + (j + 1) * P],
                        ident[:, :]), sig="ftr")
                    p.wait("vector", ev)
                    ev2 = p.emit("vector", lambda v, dc=dc, ob=ob, k=k: v.tensor_copy(
                        ob[:, dc * P:(dc + 1) * P], ptr[k % 4][:, :]), sig="fev")
                    fe_ev.append(ev2)
                p.wait("gpsimd", ev2)
                evs_ = p.dma("gpsimd", xo[j * P:(j + 1) * P, :], ob, "fst")
                stv.append(evs_)
            p.wait("gpsimd", p.cur("fst"))
            blk = st.enter_context(nc.Block())
            p.build(blk)
            return nc

        w1a = [Bb[:, 0:4096], Bb[:, 4096:8192]]
        w2a = Bb[:, 8192:12288]
        hsb = [Bb[:, 12288 + i * 512:12288 + (i + 1) * 512] for i in range(4)]
        msb = Bb[:, 14336:14848]
        tmp = Bb[:, 14848:15360]
        p.wait("sync", p.cur("x3st"))
        w1cons = {}
        w2cons = {}
        hcons = {0: None, 1: None}
        ymm_last = None
        mke_prev = None
        mrow_sl = lnst[0:1, 512:1024]
        for e in range(4):
            for fb in range(8):
                kwb = e * 8 + fb
                par = kwb % 2
                if kwb >= 2:
                    p.wait("sync", w1cons[kwb - 2])
                ew1 = p.dma("sync",
                            w1a[par].rearrange("p (dc f) -> p dc f", f=512),
                            w1_i[e, :, fb * 512:(fb + 1) * 512]
                            .rearrange("(dc p) f -> p dc f", p=P), "wld")
                if kwb >= 1:
                    p.wait("sync", w2cons[kwb - 1])
                ew2 = p.dma("sync",
                            w2a.rearrange("p (fc d) -> p fc d", d=DM),
                            w2_i[e, fb * 512:(fb + 1) * 512, :]
                            .rearrange("(fc p) d -> p fc d", p=P), "wld")
                p.wait("tensor", ew1)
                for tb in range(4):
                    if mke_prev is not None:
                        p.wait("sync", mke_prev)
                    evme = p.dma("sync", mrow_sl, mrow_d[e, tb * 512:(tb + 1) * 512],
                                 "mld")
                    p.wait("tensor", evme)
                    if mke_prev is not None:
                        p.wait("tensor", mke_prev)
                    ev = p.emit("tensor", lambda t: t.matmul(
                        pys[2][:, :], ones1[:, :], mrow_sl,
                        start=True, stop=True), sig="mkb")
                    p.wait("vector", ev)
                    evm = p.emit("vector", lambda v: v.tensor_copy(msb[:, :],
                                                                   pys[2][:, :]),
                                 sig="mke")
                    mke_prev = evm
                    evh_l = []
                    for fc in range(4):
                        slot = (tb * 4 + fc) % 2
                        if hcons[slot] is not None:
                            p.wait("tensor", hcons[slot])
                        for dc in range(8):
                            ev = p.emit("tensor", lambda t, par=par, dc=dc, fc=fc, tb=tb,
                                        slot=slot: t.matmul(
                                            pys[slot][:, :],
                                            w1a[par][:, dc * 512 + fc * P:dc * 512 + (fc + 1) * P],
                                            C[:, dc * T + tb * 512:dc * T + (tb + 1) * 512],
                                            start=(dc == 0), stop=(dc == 7)), sig="hmm")
                        p.wait("scalar", ev)
                        if ymm_last is not None:
                            p.wait("scalar", ymm_last)
                        evh = p.emit("scalar", lambda s, fc=fc, e=e, fb=fb, slot=slot:
                                     s.activation(
                                         hsb[fc], pys[slot][:, :], AF.Gelu,
                                         bias=b1t[:, e * 32 + fb * 4 + fc:
                                                  e * 32 + fb * 4 + fc + 1]),
                                     sig="hact")
                        hcons[slot] = evh
                        evh_l.append(evh)
                    w1cons[kwb] = ev
                    p.wait("tensor", evh_l[-1])
                    p.wait("tensor", ew2)
                    p.wait("vector", evm)
                    for oc in range(8):
                        for fc in range(4):
                            ev = p.emit("tensor", lambda t, oc=oc, fc=fc: t.matmul(
                                pys[3][:, :],
                                w2a[:, fc * DM + oc * P:fc * DM + (oc + 1) * P],
                                hsb[fc], start=(fc == 0), stop=(fc == 3)), sig="ymm")
                        p.wait("vector", ev)
                        dsl = A[:, oc * T + tb * 512:oc * T + (tb + 1) * 512]
                        if e == 0 and fb == 0:
                            ev2 = p.emit("vector", lambda v, dsl=dsl, e=e, oc=oc:
                                         v.scalar_tensor_tensor(
                                             dsl, pys[3][:, :],
                                             b2t[:, e * 8 + oc:e * 8 + oc + 1],
                                             msb[:, :], OP.add, OP.mult), sig="yacc")
                        else:
                            if fb == 0:
                                p.emit("vector", lambda v, e=e, oc=oc:
                                       v.scalar_tensor_tensor(
                                           tmp[:, :], pys[3][:, :],
                                           b2t[:, e * 8 + oc:e * 8 + oc + 1],
                                           msb[:, :], OP.add, OP.mult))
                            else:
                                p.emit("vector", lambda v: v.tensor_mul(
                                    tmp[:, :], pys[3][:, :], msb[:, :]))
                            ev2 = p.emit("vector", lambda v, dsl=dsl: v.tensor_add(
                                dsl, dsl, tmp[:, :]), sig="yacc")
                        p.wait("tensor", ev2)
                    ymm_last = ev
                w2cons[kwb] = ev
        p.wait("sync", p.cur("yacc"))
        fevs = []
        for dc in range(8):
            evl = p.dma("sync", Bb[:, dc * T:(dc + 1) * T],
                        x3sc[dc * P:(dc + 1) * P, :], "fld")
            p.wait("vector", evl)
            ev = p.emit("vector", lambda v, dc=dc: v.tensor_add(
                A[:, dc * T:(dc + 1) * T], A[:, dc * T:(dc + 1) * T],
                Bb[:, dc * T:(dc + 1) * T]), sig="fadd")
            fevs.append(ev)
        osb = Bb[:, 8 * T - 2 * DM:8 * T - DM]
        osb2 = Bb[:, 8 * T - DM:8 * T]
        for ev in fevs:
            p.wait("tensor", ev)
        stv = []
        fe_ev = []
        for j in range(NT):
            ob = [osb, osb2][j % 2]
            if j >= 2:
                p.wait("vector", stv[j - 2])
            for dc in range(8):
                k = j * 8 + dc
                if k >= 4:
                    p.wait("tensor", fe_ev[k - 4])
                ev = p.emit("tensor", lambda t, j=j, dc=dc, k=k: t.transpose(
                    ptr[k % 4][:, :], A[:, dc * T + j * P:dc * T + (j + 1) * P],
                    ident[:, :]), sig="ftr")
                p.wait("vector", ev)
                ev2 = p.emit("vector", lambda v, dc=dc, ob=ob, k=k: v.tensor_copy(
                    ob[:, dc * P:(dc + 1) * P], ptr[k % 4][:, :]), sig="fev")
                fe_ev.append(ev2)
            p.wait("gpsimd", ev2)
            evs = p.dma("gpsimd", xo[j * P:(j + 1) * P, :], ob, "fst")
            stv.append(evs)
        p.wait("gpsimd", p.cur("fst"))
        blk = st.enter_context(nc.Block())
        p.build(blk)
    return nc


def _ext3(rel, c):
    f32 = np.float32
    cp = np.arange(128)
    j = np.arange(4608)
    r = (cp[:, None] // 16) + j[None, :] + 3584 - 512 * c
    g0 = cp % 16
    rc = np.clip(r, 0, 2046)
    return np.ascontiguousarray(rel[rc, g0[:, None]].astype(f32))


def _consts():
    return {
        "ident": np.eye(P, dtype=np.float32),
        "ones1": np.ones((1, P), dtype=np.float32),
        "onescol": np.ones((P, 1), dtype=np.float32),
    }


def _get(name, builder):
    if name not in _cache:
        _cache[name] = builder()
    return _cache[name]


_cache = {}


def kernel(L0_attn_x, L0_attn_filt, L0_ffn_w_gate, L0_ffn_w_in, L0_ffn_w_out,
           L1_attn_Wq, L1_attn_Wk, L1_attn_Wv, L1_attn_rel_table,
           L1_attn_gamma, L1_attn_beta, L1_ffn_W1, L1_ffn_b1, L1_ffn_W2,
           L1_ffn_b2, L1_ffn_Wg, L1_ffn_bg):
    f32 = np.float32
    x = np.ascontiguousarray(L0_attn_x, f32)
    filt = np.asarray(L0_attn_filt, f32)
    C = _consts()

    fpad = np.concatenate([np.zeros(127, f32), filt, np.zeros(128, f32)])
    in_maps = []
    for c in range(8):
        b, h = divmod(c, 2)
        rows = x[b, 2048 * h:]
        if h == 1:
            rows = np.concatenate([rows, np.zeros((2048, DM), f32)], 0)
        in_maps.append({"xb": np.ascontiguousarray(rows), "fpad": fpad})
    r = run_bass_kernel_spmd(_get("conv", build_conv), in_maps, core_ids=list(range(8)))
    x1 = np.stack([np.concatenate([r.results[2 * b]["x1"], r.results[2 * b + 1]["x1"]], 0)
                   for b in range(4)])

    in_maps = [{"win": np.ascontiguousarray(L0_ffn_w_in[c], f32),
                "wout": np.ascontiguousarray(L0_ffn_w_out[c], f32),
                "ident": C["ident"]} for c in range(8)]
    r = run_bass_kernel_spmd(_get("mpre", build_mpre), in_maps, core_ids=list(range(8)))
    mts = np.stack([r.results[c]["mt"] for c in range(8)])

    in_maps = []
    for c in range(8):
        b, h = divmod(c, 2)
        in_maps.append({"x1": np.ascontiguousarray(x1[b, 2048 * h:2048 * (h + 1)]),
                        "mts": mts, "wgate": np.ascontiguousarray(L0_ffn_w_gate, f32),
                        **C})
    r = run_bass_kernel_spmd(_get("moe0", build_moe0), in_maps, core_ids=list(range(8)))
    x2f = np.stack([r.results[c]["x2f"] for c in range(8)])
    z2f = np.stack([r.results[c]["z2f"] for c in range(8)])

    rel = np.asarray(L1_attn_rel_table, f32)
    in_maps = []
    for c in range(8):
        ext3c = _ext3(rel, c)
        in_maps.append({
            "z2all": z2f,
            "wq": np.ascontiguousarray(L1_attn_Wq[:, 128 * c:128 * (c + 1)], f32),
            "wk": np.ascontiguousarray(L1_attn_Wk[:, 128 * c:128 * (c + 1)], f32),
            "wv": np.ascontiguousarray(L1_attn_Wv[:, 128 * c:128 * (c + 1)], f32),
            "ext3": ext3c, "ident": C["ident"], "ones1": C["ones1"],
        })
    r = run_bass_kernel_spmd(_get("attn", build_attn), in_maps, core_ids=list(range(8)))
    ao = np.zeros((4, 16, S, 64), f32)
    u = np.arange(512)
    b0s = u // 256
    beta1 = 255 - (u % 256)
    for c in range(8):
        res = r.results[c]["ao"]
        for hh in range(2):
            hglob = 2 * c + hh
            for qv in range(8):
                blkq = res[:, hh, qv].reshape(4, 512, 64)
                q = 16 * beta1 + (2 * qv + b0s)
                ao[:, hglob, q, :] = blkq
    aof = ao.transpose(0, 1, 3, 2).reshape(4, DM, S)

    gmt = np.ascontiguousarray(np.asarray(L1_attn_gamma, f32).reshape(8, P))
    btt = np.ascontiguousarray(np.asarray(L1_attn_beta, f32).reshape(8, P))
    in_maps = []
    for c in range(8):
        b, h = divmod(c, 2)
        in_maps.append({
            "x2f": np.ascontiguousarray(x2f[c]),
            "aof": np.ascontiguousarray(aof[b, :, 2048 * h:2048 * (h + 1)]),
            "gm": gmt, "bt": btt,
            "w1": np.ascontiguousarray(L1_ffn_W1, f32),
            "b1": np.ascontiguousarray(np.asarray(L1_ffn_b1, f32).reshape(4, 32, P)),
            "w2": np.ascontiguousarray(L1_ffn_W2, f32),
            "b2": np.ascontiguousarray(np.asarray(L1_ffn_b2, f32).reshape(4, 8, P)),
            "wg": np.ascontiguousarray(L1_ffn_Wg, f32),
            "bg": np.ascontiguousarray(np.asarray(L1_ffn_bg, f32).reshape(1, 4)),
            **C})
    r = run_bass_kernel_spmd(_get("ffn1", build_ffn1), in_maps, core_ids=list(range(8)))
    out = np.zeros((4, S, DM), f32)
    for c in range(8):
        b, h = divmod(c, 2)
        out[b, 2048 * h:2048 * (h + 1)] = r.results[c]["xo"]
    return out
